# version 08 (final)
# baseline (speedup 1.0000x reference)
import contextlib
import numpy as np

import concourse.bass as bass
import concourse.mybir as mybir
from concourse.bass_utils import run_bass_kernel_spmd

F32 = mybir.dt.float32
BF16 = mybir.dt.bfloat16
ALU = mybir.AluOpType
ACT = mybir.ActivationFunctionType
AX = mybir.AxisListType

B, S, D, H, DH, FF = 16, 2048, 1024, 16, 64, 4096
NSEQ = 2
DCH = D // 128
FCH = FF // 128
TC = 512
NTC = S // TC
NT2 = S // 2
RC = 12582912.0
EXPS = -14.0
TBC = 128

PHASES = ("attn_a", "ffn1", "attn_b", "ffn2")
DEBUG_TAPS = []
ATTNB_PARTS = ("p1", "p2", "p3")
P1SUB = ("beta", "heads", "gate", "chunks")
HEADSUB = ("norms", "kq", "stage")
P3CB = 8
P3SUB = ("rawload", "ipload", "sig", "collect", "combine", "ln", "tr", "wout")
ATTNB_DBG = False


class Prog:
    ENG = ("sync", "vector", "scalar", "tensor", "gpsimd")

    def __init__(self):
        self.ops = []
        self.cnt = {s: 0 for s in ("dma", "dmg", "dms", "pe", "dve", "act", "gp")}

    def _op(self, eng, fn, waits, inc, amt):
        for s, v in waits:
            assert v <= self.cnt[s], f"wait on future {s}:{v} > {self.cnt[s]}"
        if inc is not None:
            self.cnt[inc] += amt
        wd = {}
        for s, v in waits:
            wd[s] = max(wd.get(s, 0), v)
        self.ops.append((eng, fn, tuple(wd.items()), inc, amt))

    def v(self, fn, waits=()):
        self._op("vector", fn, waits, "dve", 1)
        return ("dve", self.cnt["dve"])

    def s(self, fn, waits=()):
        self._op("scalar", fn, waits, "act", 1)
        return ("act", self.cnt["act"])

    def t(self, fn, waits=()):
        self._op("tensor", fn, waits, "pe", 1)
        return ("pe", self.cnt["pe"])

    def g(self, fn, waits=()):
        self._op("gpsimd", fn, waits, "gp", 1)
        return ("gp", self.cnt["gp"])

    def dma(self, out, in_, waits=(), q="sync", **kw):
        sem = {"sync": "dma", "gpsimd": "dmg", "scalar": "dms"}[q]
        self._op(q, lambda e, out=out, in_=in_, kw=kw: e.dma_start(out=out, in_=in_, **kw),
                 waits, sem, 16)
        return (sem, self.cnt[sem])

    def now(self, sem):
        return (sem, self.cnt[sem])

    def barrier(self):
        return [(s, self.cnt[s]) for s in ("pe", "dve", "act", "gp", "dma", "dmg", "dms")]


def build_program(phases=PHASES):
    nc = bass.Bass("TRN2", target_bir_lowering=False, debug=False)
    p = Prog()

    x_in = nc.dram_tensor("x_in", [NSEQ, D, S], F32, kind="ExternalInput").ap()
    x_out = nc.dram_tensor("x_out", [NSEQ, D, S], F32, kind="ExternalOutput").ap()
    wqkv = nc.dram_tensor("wqkv", [D, 3 * D], F32, kind="ExternalInput").ap()
    wouta = nc.dram_tensor("wouta", [D, D], F32, kind="ExternalInput").ap()
    wbq = nc.dram_tensor("wbq", [D, D], F32, kind="ExternalInput").ap()
    wbk = nc.dram_tensor("wbk", [D, D], F32, kind="ExternalInput").ap()
    wbv = nc.dram_tensor("wbv", [D, D], F32, kind="ExternalInput").ap()
    wbbeta = nc.dram_tensor("wbbeta", [D, H], F32, kind="ExternalInput").ap()
    wbg = nc.dram_tensor("wbg", [D, D], F32, kind="ExternalInput").ap()
    wbout = nc.dram_tensor("wbout", [D, D], F32, kind="ExternalInput").ap()
    lng = nc.dram_tensor("lng", [1, D], F32, kind="ExternalInput").ap()
    lnb = nc.dram_tensor("lnb", [1, D], F32, kind="ExternalInput").ap()
    w1 = nc.dram_tensor("w1", [FF, D], F32, kind="ExternalInput").ap()
    convw = nc.dram_tensor("convw", [FF, 3], F32, kind="ExternalInput").ap()
    convb = nc.dram_tensor("convb", [FF, 1], F32, kind="ExternalInput").ap()
    w2 = nc.dram_tensor("w2", [D, FF], F32, kind="ExternalInput").ap()
    c_id = nc.dram_tensor("c_id", [128, 128], BF16, kind="ExternalInput").ap()
    c_mask = nc.dram_tensor("c_mask", [4, 128, TC], BF16, kind="ExternalInput").ap()
    c_ones1 = nc.dram_tensor("c_ones1", [1, 128], F32, kind="ExternalInput").ap()
    c_onesb = nc.dram_tensor("c_onesb", [128, 3], BF16, kind="ExternalInput").ap()
    c_vals = nc.dram_tensor("c_vals", [1, 12], F32, kind="ExternalInput").ap()
    c_sel2 = nc.dram_tensor("c_sel2", [2, 256], F32, kind="ExternalInput").ap()
    s_w1q = nc.dram_tensor("s_w1q", [FCH, DCH, 128, 128], BF16, kind="Internal").ap()
    s_w2q = nc.dram_tensor("s_w2q", [DCH, FCH, 128, 128], BF16, kind="Internal").ap()
    s_misc = nc.dram_tensor("s_misc", [128, 64], F32, kind="Internal").ap()
    _KND = "ExternalOutput" if ATTNB_DBG else "Internal"
    s_kp = nc.dram_tensor("s_kp", [2, 8, 3, NT2, DH], BF16, kind=_KND).ap()
    s_ipa = nc.dram_tensor("s_ipa", [8, 2, 3, NT2, DH], BF16, kind=_KND).ap()
    s_ipb = nc.dram_tensor("s_ipb", [8, 2, 3, NT2, DH], BF16, kind=_KND).ap()
    s_qv = nc.dram_tensor("s_qv", [3, 16, 64, NT2], BF16, kind=_KND).ap()
    s_r = nc.dram_tensor("s_r", [NT2, 48, 512], BF16, kind=_KND).ap()
    s_nr = nc.dram_tensor("s_nr", [16, 3, S], BF16, kind=_KND).ap()
    s_cd = nc.dram_tensor("s_cd", [16, NT2], BF16, kind=_KND).ap()
    s_gate = nc.dram_tensor("s_gate", [DCH, 128, S], BF16, kind=_KND).ap()
    s_qfm = nc.dram_tensor("s_qfm", [16, 64, S], BF16, kind=_KND).ap()
    s_kfm = nc.dram_tensor("s_kfm", [16, 64, S], BF16, kind=_KND).ap()
    s_vfm = nc.dram_tensor("s_vfm", [16, 64, S], BF16, kind=_KND).ap()
    if ATTNB_DBG:
        dbg_tok = nc.dram_tensor("dbg_tok", [8, TBC, 2 * 1024], BF16, kind="ExternalOutput").ap()
        dbg_ofm = nc.dram_tensor("dbg_ofm", [4, 128, DCH * TC], BF16, kind="ExternalOutput").ap()
        dbg_tkr = nc.dram_tensor("dbg_tkr", [8, TBC, 3 * 1024], BF16, kind="ExternalOutput").ap()
        dbg_sct = nc.dram_tensor("dbg_sct", [8, TBC, 160], BF16, kind="ExternalOutput").ap()
        dbg_tk2 = nc.dram_tensor("dbg_tk2", [8, TBC, 2 * 1024], BF16, kind="ExternalOutput").ap()
        dbg_lnc = nc.dram_tensor("dbg_lnc", [8, TBC, 8], F32, kind="ExternalOutput").ap()
        dbg_ctp = nc.dram_tensor("dbg_ctp", [TBC, 1024], F32, kind="ExternalOutput").ap()
    dbg = {}
    for nm_ in DEBUG_TAPS:
        shp = {"xhat": [128, DCH * S], "qfm": [64, S], "kfm": [64, S], "vfm": [64, S],
               "vT": [128, 16 * 128], "attnout": [128, DCH * S], "pT": [128, 2 * TC],
               "muB": [128, S], "rsB": [128, S], "hwin": [128, FCH * 520],
               "xqs": [128, DCH * S], "hs": [128, DCH * S]}[nm_]
        dtt = BF16 if nm_ != "muB" and nm_ != "rsB" else BF16
        dbg[nm_] = nc.dram_tensor("dbg_" + nm_, shp, dtt, kind="ExternalOutput").ap()

    def tap(nm_, src_ap):
        if nm_ in dbg:
            p.dma(dbg[nm_], src_ap, waits=p.barrier())

    es_global = contextlib.ExitStack()
    _uid = [0]

    def mk(stack, name, shape, dt):
        _uid[0] += 1
        return stack.enter_context(nc.sbuf_tensor(f"{name}_{_uid[0]}", shape, dt))

    def mkp(stack, name, shape, dt=F32):
        _uid[0] += 1
        return stack.enter_context(nc.psum_tensor(f"{name}_{_uid[0]}", shape, dt))

    G = es_global
    xres = mk(G, "xres", [128, DCH * S], F32)
    xhat = mk(G, "xhat", [128, DCH * S], BF16)
    idt = mk(G, "idt", [128, 128], BF16)
    ones1 = mk(G, "ones1", [1, 128], F32)
    onesb = mk(G, "onesb", [128, 3], BF16)
    lngv = mk(G, "lngv", [1, D], F32)
    lnbv = mk(G, "lnbv", [1, D], F32)
    mw1 = mk(G, "mw1", [128, 1], F32)
    mw2 = mk(G, "mw2", [128, 1], F32)
    convW = mk(G, "convW", [128, FCH * 4], F32)
    lnrow = mk(G, "lnrow", [1, 2 * TC], F32)
    smalls = mk(G, "smalls", [128, 2 * TC], F32)
    qtmp = mk(G, "qtmp", [128, S], F32)

    def xr(c):
        return xres[:, S * c:S * c + S]

    def xh(c):
        return xhat[:, S * c:S * c + S]

    cvt = mk(G, "cvt", [128, 12], F32)
    CONST_VALS = [1e-5, -14.0, 0.125, 1.0, 0.0, 1e-12, 127.0, 1.0 / 127.0, 1.0 / 1024.0, -0.5, 0, 0]
    p.dma(cvt[:], c_vals.partition_broadcast(128))
    for _i, _v in enumerate(CONST_VALS):
        nc.const_aps.register(F32, _v, cvt[:, _i:_i + 1])
    sel2 = mk(G, "sel2", [2, 256], F32)
    t_sel = p.dma(sel2[:], c_sel2)
    p.dma(idt[:], c_id)
    p.dma(ones1[:], c_ones1)
    p.dma(onesb[:], c_onesb)
    p.dma(lngv[:], lng)
    p.dma(lnbv[:], lnb)
    p.dma(convW[:].rearrange("q (f k) -> q f k", f=FCH)[:, :, 0:3],
          convw.rearrange("(f q) k -> q f k", q=128), allow_slow_non_contiguous=True)
    p.dma(convW[:].rearrange("q (f k) -> q f k", f=FCH)[:, :, 3:4],
          convb.rearrange("(f q) k -> q f k", q=128), allow_slow_non_contiguous=True)
    gamB = mk(G, "gamB", [128, D], BF16)
    betB = mk(G, "betB", [128, D], BF16)
    t_const = p.now("dma")

    def prep_ffn_weights2():
        with contextlib.ExitStack() as ps:
            ptr = mkp(ps, "ptr", [128, 2 * 128], BF16)
            wstage = mk(ps, "wstage", [128, 2 * 1024], F32)
            wcvt = mk(ps, "wcvt", [128, 3 * 1024], BF16)
            acc = mk(ps, "acc", [128, 2], F32)
            accr = mk(ps, "accr", [1, 128], F32)
            scal = mk(ps, "scal", [128, 1], F32)
            for wt, s_wt, mwv, nch, ncols in ((w1, s_w1q, mw1, FCH, D), (w2, s_w2q, mw2, DCH, FF)):
                nld = ncols // 1024
                tv = None
                for c in range(nch):
                    for sc_ in range(nld):
                        pp = (c * nld + sc_) % 2
                        ld = p.dma(wstage[:, 1024 * pp:1024 * pp + 1024],
                                   wt[128 * c:128 * c + 128, 1024 * sc_:1024 * sc_ + 1024],
                                   waits=[tv] if tv else [t_const])
                        ta = p.s(lambda e, pp=pp: e.activation(
                            qtmp[:, 0:1024], wstage[:, 1024 * pp:1024 * pp + 1024],
                            ACT.Abs, accum_out=acc[:, 1:2]), waits=[ld])
                        if c == 0 and sc_ == 0:
                            tv = p.v(lambda e: e.tensor_copy(acc[:, 0:1], acc[:, 1:2]), waits=[ta])
                        else:
                            tv = p.v(lambda e: e.tensor_tensor(acc[:, 0:1], acc[:, 0:1], acc[:, 1:2], ALU.add), waits=[ta])
                d1 = p.dma(s_misc[:, 0:1], acc[:, 0:1], waits=[tv], q="gpsimd",
                           allow_slow_non_contiguous=True)
                d2 = p.dma(accr[:], s_misc[:, 0:1].rearrange("p n -> n p"), waits=[d1], q="gpsimd",
                           allow_slow_non_contiguous=True)
                tv = p.v(lambda e: e.tensor_reduce(acc[0:1, 0:1], accr[:], AX.X, ALU.add), waits=[d2])
                tv = p.v(lambda e, nch=nch, ncols=ncols: e.tensor_scalar(
                    acc[0:1, 0:1], acc[0:1, 0:1], 1.0 / (nch * 128 * ncols), 1e-5, ALU.mult, ALU.max), waits=[tv])
                tv2 = p.v(lambda e: e.reciprocal(acc[0:1, 1:2], acc[0:1, 0:1]), waits=[tv])
                d3 = p.dma(s_misc[0:1, 2:4], acc[0:1, 0:2], waits=[tv2], q="gpsimd")
                d4 = p.dma(mwv[:], s_misc[0:1, 2:3].partition_broadcast(128), waits=[d3], q="gpsimd")
                d5 = p.dma(scal[:], s_misc[0:1, 3:4].partition_broadcast(128), waits=[d3], q="gpsimd")
                prev_t3 = None
                for c in range(nch):
                    for sc_ in range(nld):
                        pp = (c * nld + sc_) % 2
                        ld = p.dma(wstage[:, 1024 * pp:1024 * pp + 1024],
                                   wt[128 * c:128 * c + 128, 1024 * sc_:1024 * sc_ + 1024],
                                   waits=[d4, d5] + ([prev_t3] if prev_t3 else []))
                        t1 = p.v(lambda e, pp=pp: e.tensor_scalar(
                            qtmp[:, 0:1024], wstage[:, 1024 * pp:1024 * pp + 1024], scal[:, 0:1], None, ALU.mult),
                            waits=[ld])
                        t2 = p.v(lambda e: e.tensor_scalar(qtmp[:, 0:1024], qtmp[:, 0:1024], RC, RC, ALU.add, ALU.subtract))
                        prev_t3 = p.v(lambda e, pp=pp: e.tensor_scalar(
                            wcvt[:, 1024 * pp:1024 * pp + 1024], qtmp[:, 0:1024], 1.0, -1.0, ALU.min, ALU.max))
                        for b_ in range(8):
                            tt = p.t(lambda e, pp=pp, b_=b_: e.transpose(
                                ptr[:, 128 * (b_ % 2):128 * (b_ % 2) + 128],
                                wcvt[:, 1024 * pp + 128 * b_:1024 * pp + 128 * b_ + 128], idt[:]),
                                waits=[prev_t3] + ([p.now("act")] if b_ >= 2 else []))
                            ev = p.s(lambda e, b_=b_: e.copy(
                                wcvt[:, 2048 + 128 * b_:2048 + 128 * b_ + 128],
                                ptr[:, 128 * (b_ % 2):128 * (b_ % 2) + 128]),
                                waits=[tt, p.now("dma")])
                            if wt is w1:
                                dst = s_w1q[c, 8 * sc_ + b_]
                            else:
                                dst = s_w2q[c, 8 * sc_ + b_]
                            p.dma(dst, wcvt[:, 2048 + 128 * b_:2048 + 128 * b_ + 128], waits=[ev])
        return p.barrier()

    def layernorm2(wait0):
        lstk = contextlib.ExitStack()
        psl = mkp(lstk, "psl", [128, 2 * TC])
        muB = mk(lstk, "muB", [128, S], BF16)
        rsB = mk(lstk, "rsB", [128, S], BF16)
        xb16 = qtmp.bitcast(BF16)
        toks = []
        for tc in range(NTC):
            for c in range(DCH):
                cb = p.s(lambda e, c=c, tc=tc: e.copy(xb16[:, 0:TC], xr(c)[:, TC * tc:TC * tc + TC]),
                         waits=list(wait0) + [p.now("pe"), p.now("dve")])
                sq = p.s(lambda e, c=c, tc=tc: e.activation(xb16[:, TC:2 * TC],
                                                            xr(c)[:, TC * tc:TC * tc + TC], ACT.Square))
                p.t(lambda e, c=c, tc=tc: e.matmul(
                    psl[0:1, 0:TC], onesb[:, 0:1], xb16[:, 0:TC],
                    start=(c == 0), stop=(c == DCH - 1)), waits=[cb, sq])
                p.t(lambda e, c=c, tc=tc: e.matmul(
                    psl[0:1, TC:2 * TC], onesb[:, 0:1], xb16[:, TC:2 * TC],
                    start=(c == 0), stop=(c == DCH - 1)))
            tm = p.now("pe")
            ev = p.s(lambda e: e.copy(smalls[0:1, 0:2 * TC], psl[0:1, 0:2 * TC]), waits=[tm])
            mu = smalls[0:1, 0:TC]
            vr = smalls[0:1, TC:2 * TC]
            v1 = p.v(lambda e: e.tensor_scalar(mu, mu, 1.0 / D, None, ALU.mult), waits=[ev])
            v2 = p.v(lambda e: e.tensor_scalar(vr, vr, 1.0 / D, None, ALU.mult))
            v3 = p.v(lambda e: e.scalar_tensor_tensor(qtmp[0:1, 0:TC], mu, 1.0, mu, ALU.mult, ALU.mult))
            v4 = p.v(lambda e: e.tensor_tensor(vr, vr, qtmp[0:1, 0:TC], ALU.subtract))
            s1 = p.s(lambda e: e.activation(vr, vr, ACT.Sqrt, bias=1e-5), waits=[v4])
            v5 = p.v(lambda e: e.reciprocal(vr, vr), waits=[s1])
            b1 = p.t(lambda e: e.matmul(psl[:, 0:TC], sel2[:, 0:128], smalls[0:2, 0:TC],
                                        start=True, stop=True), waits=[v5, t_sel])
            e1 = p.s(lambda e, tc=tc: e.copy(muB[:, TC * tc:TC * tc + TC], psl[:, 0:TC]), waits=[b1])
            b2 = p.t(lambda e: e.matmul(psl[:, TC:2 * TC], sel2[:, 0:128], smalls[0:2, TC:2 * TC],
                                        start=True, stop=True))
            e2 = p.s(lambda e, tc=tc: e.copy(rsB[:, TC * tc:TC * tc + TC], psl[:, TC:2 * TC]), waits=[b2])
            toks.append(e2)
        tb = p.now("act")
        outs = None
        for c in range(DCH):
            p.v(lambda e, c=c: e.tensor_tensor(qtmp[:], xr(c), muB[:], ALU.subtract), waits=[tb])
            outs = p.v(lambda e, c=c: e.tensor_tensor(xh(c), qtmp[:], rsB[:], ALU.mult))
        lstk.close()
        return outs

    def attn_a(seq, wait0):
        with contextlib.ExitStack() as stk:
            t_ln = layernorm2(wait0)
            wstage = mk(stk, "wstage", [128, 2 * 1024], F32)
            wqkb = mk(stk, "wqkb", [128, DCH * 128], BF16)
            wvb = mk(stk, "wvb", [128, DCH * DH], BF16)
            qfm = mk(stk, "qfm", [64, S], BF16)
            kfm = mk(stk, "kfm", [64, S], BF16)
            vfm = mk(stk, "vfm", [64, S], BF16)
            vT = mk(stk, "vT", [128, 16 * 128], BF16)
            pT = mk(stk, "pT", [128, 2 * TC], BF16)
            attnout = mk(stk, "attnout", [128, DCH * S], BF16)
            denr = smalls[64:65, 0:TC]
            recipd = smalls[64:65, TC:2 * TC]
            recipB = mk(stk, "recipB", [64, TC], F32)
            selg = mk(stk, "selg", [128, 64], F32)
            o64 = p.dma(selg[64:66, :], c_sel2.rearrange("a (b n) -> a b n", b=4)[:, 0, 0:64],
                        waits=[p.now("act"), p.now("pe"), p.now("dve")])
            wob = mk(stk, "wob", [128, DCH * 128], BF16)
            maskt = mk(stk, "maskt", [128, 4 * TC], BF16)
            lmask = p.dma(maskt[:].rearrange("q (m n) -> q m n", m=4),
                          c_mask.rearrange("m q n -> q m n"), waits=[p.now("pe"), p.now("act"), p.now("dve")])
            psp = mkp(stk, "psp", [128, 2 * TC])
            pss = mkp(stk, "pss", [128, 2 * TC])
            psv = mkp(stk, "psv", [128, TC])
            psr = mkp(stk, "psr", [64, TC])
            ones_set = p.v(lambda e: e.memset(vT[:].rearrange("q (k n) -> q k n", k=16)[:, :, 64:65], 1.0),
                           waits=[p.now("act"), p.now("pe")])
            for h in range(H):
                l1 = p.dma(wstage[:].rearrange("q (c n) -> q c n", c=2 * DCH)[:, 0:DCH, 0:64],
                           wqkv.rearrange("(c q) m -> q c m", q=128)[:, :, 64 * h:64 * h + 64],
                           waits=[p.now("act"), p.now("dve")])
                l2 = p.dma(wstage[:].rearrange("q (c n) -> q c n", c=2 * DCH)[:, 0:DCH, 64:128],
                           wqkv.rearrange("(c q) m -> q c m", q=128)[:, :, D + 64 * h:D + 64 * h + 64])
                l3 = p.dma(wstage[:].rearrange("q (c n) -> q c n", c=2 * DCH)[:, DCH:2 * DCH, 0:64],
                           wqkv.rearrange("(c q) m -> q c m", q=128)[:, :, 2 * D + 64 * h:2 * D + 64 * h + 64])
                cv1 = p.v(lambda e: e.tensor_copy(wqkb[:], wstage[:, 0:1024]), waits=[l1, l2, p.now("pe")])
                cv2 = p.v(lambda e: e.tensor_copy(wvb[:], wstage[:].rearrange(
                    "q (c n) -> q c n", c=2 * DCH)[:, DCH:2 * DCH, 0:64]), waits=[l3])
                for tc in range(NTC):
                    pp = tc % 2
                    for c in range(DCH):
                        p.t(lambda e, tc=tc, c=c, pp=pp: e.matmul(
                            psp[:, TC * pp:TC * pp + TC],
                            wqkb[:].rearrange("q (c n) -> q c n", c=DCH)[:, c, :],
                            xh(c)[:, TC * tc:TC * tc + TC],
                            start=(c == 0), stop=(c == DCH - 1)),
                            waits=[cv1, t_ln] + ([p.now("act")] if tc >= 2 else []))
                    tm = p.now("pe")
                    p.s(lambda e, tc=tc, pp=pp: e.activation(qfm[:, TC * tc:TC * tc + TC],
                                                             psp[0:64, TC * pp:TC * pp + TC],
                                                             ACT.Copy, scale=0.125), waits=[tm])
                    p.s(lambda e, tc=tc, pp=pp: e.copy(kfm[:, TC * tc:TC * tc + TC],
                                                       psp[64:128, TC * pp:TC * pp + TC]), waits=[tm])
                t_qk = p.now("act")
                for tc in range(NTC):
                    pp = tc % 2
                    for c in range(DCH):
                        p.t(lambda e, tc=tc, c=c, pp=pp: e.matmul(
                            psp[0:64, TC * pp:TC * pp + TC],
                            wvb[:].rearrange("q (c n) -> q c n", c=DCH)[:, c, :],
                            xh(c)[:, TC * tc:TC * tc + TC],
                            start=(c == 0), stop=(c == DCH - 1)),
                            waits=[cv2] + ([p.now("act")] if tc >= 2 else []))
                    tm = p.now("pe")
                    p.s(lambda e, tc=tc, pp=pp: e.copy(vfm[:, TC * tc:TC * tc + TC],
                                                       psp[0:64, TC * pp:TC * pp + TC]), waits=[tm])
                t_v = p.now("act")
                tts = []
                tprev_pe = p.now("pe")
                for kt in range(16):
                    dd = p.dma(vT[:].rearrange("q (k n) -> q k n", k=16)[:, kt, 0:64],
                               vfm[:, 128 * kt:128 * kt + 128], waits=[t_v, tprev_pe], transpose=True)
                    tts.append(dd)
                for qc in range(NTC):
                    nkt = 4 * qc + 4
                    for kt in range(nkt):
                        pp = kt % 2
                        diag = (kt // 4 == qc)
                        w = [t_qk] + ([p.now("act")] if kt >= 2 else [])
                        p.t(lambda e, kt=kt, qc=qc, pp=pp, diag=diag: e.matmul(
                            pss[:, TC * pp:TC * pp + TC],
                            kfm[:, 128 * kt:128 * kt + 128],
                            qfm[:, TC * qc:TC * qc + TC],
                            start=True, stop=not diag), waits=w)
                        if diag:
                            p.t(lambda e, kt=kt, pp=pp: e.matmul(
                                pss[:, TC * pp:TC * pp + TC], idt[:],
                                maskt[:].rearrange("q (m n) -> q m n", m=4)[:, kt % 4, :],
                                start=False, stop=True), waits=[lmask])
                        ex = p.s(lambda e, pp=pp: e.activation(
                            pT[:, TC * pp:TC * pp + TC], pss[:, TC * pp:TC * pp + TC],
                            ACT.Exp, bias=EXPS), waits=[p.now("pe")] + (tts if kt == 0 else []))
                        p.t(lambda e, kt=kt, pp=pp, nkt=nkt: e.matmul(
                            psv[0:65, 0:TC],
                            vT[:].rearrange("q (k n) -> q k n", k=16)[:, kt, 0:65],
                            pT[:, TC * pp:TC * pp + TC],
                            start=(kt == 0), stop=(kt == nkt - 1)),
                            waits=[ex, ones_set, tts[-1]] + ([p.now("dve")] if kt == 0 else []))
                    t_pv = p.now("pe")
                    ed = p.s(lambda e: e.copy(denr, psv[64:65, 0:TC]), waits=[t_pv, p.now("dve")])
                    vr = p.v(lambda e: e.reciprocal(recipd, denr), waits=[ed])
                    br = p.t(lambda e: e.matmul(psr[:, 0:TC], selg[64:66, :], smalls[64:66, TC:2 * TC],
                                                start=True, stop=True), waits=[vr, o64, p.now("act")])
                    er = p.s(lambda e: e.copy(recipB[:], psr[:, 0:TC]), waits=[br])
                    p.v(lambda e, h=h, qc=qc: e.tensor_tensor(
                        attnout[64 * (h % 2):64 * (h % 2) + 64, S * (h // 2) + TC * qc:S * (h // 2) + TC * qc + TC],
                        psv[0:64, 0:TC], recipB[:], ALU.mult), waits=[er])
            tap("xhat", xhat[:])
            tap("qfm", qfm[:])
            tap("kfm", kfm[:])
            tap("vfm", vfm[:])
            tap("vT", vT[:])
            tap("attnout", attnout[:])
            t_heads = p.now("dve")
            for mc in range(DCH):
                cv = None
                for fc in range(DCH):
                    pp = fc % 2
                    l = p.dma(wstage[:, 128 * pp:128 * pp + 128],
                              wouta[128 * fc:128 * fc + 128, 128 * mc:128 * mc + 128],
                              waits=[p.now("act"), p.now("dve")])
                    cv = p.v(lambda e, fc=fc, pp=pp: e.tensor_copy(
                        wob[:].rearrange("q (c n) -> q c n", c=DCH)[:, fc, :],
                        wstage[:, 128 * pp:128 * pp + 128]),
                        waits=[l] + ([p.now("pe")] if fc == 0 else []))
                for tc in range(NTC):
                    pp = tc % 2
                    for fc in range(DCH):
                        p.t(lambda e, tc=tc, fc=fc, pp=pp: e.matmul(
                            psp[:, TC * pp:TC * pp + TC],
                            wob[:].rearrange("q (c n) -> q c n", c=DCH)[:, fc, :],
                            attnout[:, S * fc + TC * tc:S * fc + TC * tc + TC],
                            start=(fc == 0), stop=(fc == DCH - 1)),
                            waits=[cv, t_heads] + ([p.now("dve")] if tc >= 2 else []))
                    tm = p.now("pe")
                    p.v(lambda e, mc=mc, tc=tc, pp=pp: e.tensor_tensor(
                        xr(mc)[:, TC * tc:TC * tc + TC],
                        psp[:, TC * pp:TC * pp + TC],
                        xr(mc)[:, TC * tc:TC * tc + TC], ALU.add), waits=[tm])
            return p.barrier()

    WINDOWS = [(0, 511, 0), (511, 512, 1), (1023, 512, 2), (1535, 257, 3), (1792, 256, 3)]

    def pmax_reduce(mall, n, sc2B_, sc2BI_, mwrow, psb):
        ktot = 128
        toks = []
        tprev = p.now("dve")
        while ktot > 1:
            k = ktot // 2
            d = p.dma(qtmp.bitcast(BF16)[0:k, 0:n], mall[k:2 * k, 0:n], waits=[tprev], q="gpsimd")
            tprev = p.v(lambda e, k=k, n=n: e.tensor_tensor(mall[0:k, 0:n], mall[0:k, 0:n],
                                                            qtmp.bitcast(BF16)[0:k, 0:n], ALU.max), waits=[d])
            ktot = k
        r0 = smalls[0:1, 0:TC]
        r1 = smalls[0:1, TC:2 * TC]
        tv = p.v(lambda e, n=n: e.tensor_scalar(r0[:, 0:n], mall[0:1, 0:n], 1e-5, None, ALU.max), waits=[tprev])
        tv = p.v(lambda e, n=n: e.reciprocal(r1[:, 0:n], r0[:, 0:n]), waits=[tv])
        tv = p.v(lambda e, n=n: e.tensor_scalar(r1[:, 0:n], r1[:, 0:n], 127.0, None, ALU.mult))
        tv2 = p.v(lambda e, n=n, mwrow=mwrow: e.tensor_scalar(r0[:, 0:n], r0[:, 0:n], mwrow[0:1, 0:1],
                                                              1.0 / 127.0, ALU.mult, ALU.mult), waits=[tv])
        b1 = p.t(lambda e, n=n: e.matmul(psb[:, 0:TC][:, 0:n], sel2[:, 0:128],
                 smalls[0:2, TC:TC + n], start=True, stop=True),
                 waits=[tv2, t_sel, p.now("act"), p.now("dve")])
        e1 = p.s(lambda e, n=n, sc2B_=sc2B_: e.copy(sc2B_[:, 0:n], psb[:, 0:TC][:, 0:n]), waits=[b1])
        b2 = p.t(lambda e, n=n: e.matmul(psb[:, TC:2 * TC][:, 0:n], sel2[:, 0:128], smalls[0:2, 0:n],
                                         start=True, stop=True))
        e2 = p.s(lambda e, n=n, sc2BI_=sc2BI_: e.copy(sc2BI_[:, 0:n], psb[:, TC:2 * TC][:, 0:n]), waits=[b2])
        return e1, e2

    def ffn(seq, wait0, first):
        with contextlib.ExitStack() as stk:
            t_ln = layernorm2(wait0)
            xqs = mk(stk, "xqs", [128, DCH * S], BF16)
            mall = mk(stk, "mall", [128, TC], BF16)
            mnw = mk(stk, "mnw", [128, TC], BF16)
            hwin = mk(stk, "hwin", [128, FCH * 520], BF16)
            ct1 = qtmp[:, TC:2 * TC]
            ct2 = qtmp[:, 2 * TC:3 * TC]
            w1rd = mk(stk, "w1rd", [128, 2 * DCH * 128], BF16)
            w2rd = mk(stk, "w2rd", [128, 2 * DCH * 128], BF16)
            sc2B = mk(stk, "sc2B", [128, TC], F32)
            sc2BI = mk(stk, "sc2BI", [128, TC], F32)
            psb = mkp(stk, "psb", [128, 2 * TC])
            psf = mkp(stk, "psf", [128, 2 * TC])
            psg = mkp(stk, "psg", [128, 2 * TC])
            hv = hwin[:].rearrange("q (f n) -> q f n", f=FCH)
            hsv = xhat[:].rearrange("q (f n) -> q f n", f=FCH)

            t_xq_last = None
            for tc in range(NTC):
                tv = p.v(lambda e, tc=tc: e.tensor_copy(mall[:, 0:TC], xh(0)[:, TC * tc:TC * tc + TC]),
                         waits=[t_ln, p.now("act"), p.now("pe")])
                tv2 = p.v(lambda e, tc=tc: e.tensor_copy(mnw[:, 0:TC], xh(0)[:, TC * tc:TC * tc + TC]))
                for c in range(1, DCH):
                    tv = p.v(lambda e, c=c, tc=tc: e.tensor_tensor(mall[:, 0:TC], mall[:, 0:TC],
                                                                   xh(c)[:, TC * tc:TC * tc + TC], ALU.max))
                    tv2 = p.v(lambda e, c=c, tc=tc: e.tensor_tensor(mnw[:, 0:TC], mnw[:, 0:TC],
                                                                    xh(c)[:, TC * tc:TC * tc + TC], ALU.min))
                tv3 = p.v(lambda e: e.tensor_scalar(mnw[:, 0:TC], mnw[:, 0:TC], -1.0, None, ALU.mult))
                p.v(lambda e: e.tensor_tensor(mall[:, 0:TC], mall[:, 0:TC], mnw[:, 0:TC], ALU.max))
                e1, e2 = pmax_reduce(mall, TC, sc2B, sc2BI, mw1, psb)
                for c in range(DCH):
                    tq = p.v(lambda e, c=c, tc=tc: e.tensor_tensor(
                        qtmp[:, 0:TC], xh(c)[:, TC * tc:TC * tc + TC], sc2B[:, 0:TC], ALU.mult),
                        waits=[e1, e2])
                    p.v(lambda e: e.tensor_scalar(qtmp[:, 0:TC], qtmp[:, 0:TC], RC, RC, ALU.add, ALU.subtract))
                    t_xq_last = p.v(lambda e, c=c, tc=tc: e.tensor_tensor(
                        xqs[:, S * c + TC * tc:S * c + TC * tc + TC], qtmp[:, 0:TC], sc2BI[:, 0:TC], ALU.mult))
            t_xqs = p.now("dve")

            z1 = p.v(lambda e: e.memset(hv[:, :, 0:2], 0.0), waits=[p.now("act")])
            z2 = p.v(lambda e: e.memset(hv[:, :, 514:520], 0.0))

            wi = 0
            for tc in range(NTC):
                if tc > 0:
                    zsh = p.dma(hv[:, :, 0:2], hv[:, :, 512:514], q="gpsimd",
                                waits=[p.now("act"), p.now("dve")],
                                allow_slow_non_contiguous=True)
                else:
                    zsh = z1
                for mf in range(FCH):
                    pp = mf % 2
                    ldw = p.dma(w1rd[:].rearrange("q (r c n) -> q r c n", r=2, c=DCH)[:, pp, :, :],
                                s_w1q[mf].rearrange("c q n -> q c n"),
                                waits=([p.now("pe")] if (mf >= 2 or tc > 0) else [prep_tok, p.now("pe"), p.now("act"), p.now("dve")]))
                    for c in range(DCH):
                        p.t(lambda e, mf=mf, c=c, pp=pp, tc=tc: e.matmul(
                            psf[:, TC * pp:TC * pp + TC],
                            w1rd[:].rearrange("q (r c n) -> q r c n", r=2, c=DCH)[:, pp, c, :],
                            xqs[:, S * c + TC * tc:S * c + TC * tc + TC],
                            start=(c == 0), stop=(c == DCH - 1)),
                            waits=[ldw, t_xqs] + ([p.now("act")] if mf >= 2 else [zsh]))
                    tm = p.now("pe")
                    p.s(lambda e, mf=mf, pp=pp: e.copy(hv[:, mf, 2:514], psf[:, TC * pp:TC * pp + TC]),
                        waits=[tm])
                t_h = p.now("act")
                while wi < len(WINDOWS) and WINDOWS[wi][2] == tc:
                    s0, n, _ = WINDOWS[wi]
                    ofs = s0 - tc * TC + 2
                    for mf in range(FCH):
                        o1 = p.s(lambda e, mf=mf, ofs=ofs, n=n: e.activation(
                            ct1[:, 0:n], hv[:, mf, ofs - 1:ofs - 1 + n], ACT.Copy,
                            scale=convW[:].rearrange("q (f k) -> q f k", f=FCH)[:, mf, 0:1]),
                            waits=[t_h, z2, p.now("dve")])
                        o2 = p.v(lambda e, mf=mf, ofs=ofs, n=n: e.scalar_tensor_tensor(
                            ct2[:, 0:n], hv[:, mf, ofs:ofs + n],
                            convW[:].rearrange("q (f k) -> q f k", f=FCH)[:, mf, 1:2],
                            ct1[:, 0:n], ALU.mult, ALU.add), waits=[o1])
                        o3 = p.v(lambda e, mf=mf, ofs=ofs, n=n: e.scalar_tensor_tensor(
                            ct1[:, 0:n], hv[:, mf, ofs + 1:ofs + 1 + n],
                            convW[:].rearrange("q (f k) -> q f k", f=FCH)[:, mf, 2:3],
                            ct2[:, 0:n], ALU.mult, ALU.add))
                        sl = p.s(lambda e, mf=mf, n=n: e.activation(
                            hsv[:, mf, 0:n], ct1[:, 0:n], ACT.Silu,
                            bias=convW[:].rearrange("q (f k) -> q f k", f=FCH)[:, mf, 3:4]), waits=[o3])
                        if mf == 0:
                            p.v(lambda e, n=n: e.tensor_copy(mall[:, 0:n], hsv[:, 0, 0:n]), waits=[sl])
                            p.v(lambda e, n=n: e.tensor_copy(mnw[:, 0:n], hsv[:, 0, 0:n]))
                        else:
                            p.v(lambda e, mf=mf, n=n: e.tensor_tensor(mall[:, 0:n], mall[:, 0:n],
                                                                      hsv[:, mf, 0:n], ALU.max), waits=[sl])
                            p.v(lambda e, mf=mf, n=n: e.tensor_tensor(mnw[:, 0:n], mnw[:, 0:n],
                                                                      hsv[:, mf, 0:n], ALU.min))
                    p.v(lambda e, n=n: e.tensor_scalar(mnw[:, 0:n], mnw[:, 0:n], -1.0, None, ALU.mult))
                    p.v(lambda e, n=n: e.tensor_tensor(mall[:, 0:n], mall[:, 0:n], mnw[:, 0:n], ALU.max))
                    e1, e2 = pmax_reduce(mall, n, sc2B, sc2BI, mw2, psb)
                    for mf in range(FCH):
                        tq = p.v(lambda e, mf=mf, n=n: e.tensor_tensor(ct2[:, 0:n], hsv[:, mf, 0:n],
                                                                       sc2B[:, 0:n], ALU.mult), waits=[e1, e2])
                        p.v(lambda e, mf=mf, n=n: e.tensor_scalar(hsv[:, mf, 0:n], ct2[:, 0:n],
                                                                  RC, RC, ALU.add, ALU.subtract))
                    t_q2 = p.now("dve")
                    for md in range(DCH):
                        for kb_ in range(4):
                            pp = kb_ % 2
                            ldw = p.dma(w2rd[:].rearrange("q (r c n) -> q r c n", r=2, c=DCH)[:, pp, :, :],
                                        s_w2q[md, 8 * kb_:8 * kb_ + 8].rearrange("c q n -> q c n"),
                                        waits=([p.now("pe")] if (kb_ >= 2 or md > 0 or wi > 0) else [prep_tok, p.now("pe"), p.now("act"), p.now("dve")]))
                            for kf8 in range(8):
                                kf = 8 * kb_ + kf8
                                p.t(lambda e, md=md, kf=kf, kf8=kf8, pp=pp, n=n: e.matmul(
                                    psg[:, TC * (md % 2):TC * (md % 2) + TC][:, 0:n],
                                    w2rd[:].rearrange("q (r c n) -> q r c n", r=2, c=DCH)[:, pp, kf8, :],
                                    hsv[:, kf, 0:n],
                                    start=(kf == 0), stop=(kf == FCH - 1)),
                                    waits=[ldw, t_q2] + ([p.now("dve")] if md >= 2 else []))
                        tm = p.now("pe")
                        p.v(lambda e, md=md, n=n: e.tensor_tensor(
                            ct2[:, 0:n], psg[:, TC * (md % 2):TC * (md % 2) + TC][:, 0:n],
                            sc2BI[:, 0:n], ALU.mult), waits=[tm])
                        p.v(lambda e, md=md, s0=s0, n=n: e.tensor_tensor(
                            xr(md)[:, s0:s0 + n], ct2[:, 0:n], xr(md)[:, s0:s0 + n], ALU.add))
                    wi += 1
            tap("hwin", hwin[:])
            tap("xqs", xqs[:])
            tap("hs", xhat[:])
            return p.barrier()

    TSP = 16

    def attn_b_p1(seq, wait0):
        with contextlib.ExitStack() as stk:
            t_ln = layernorm2(wait0)
            wstage = mk(stk, "wstage", [128, 2 * 1024], F32)
            wqkb = mk(stk, "wqkb", [128, DCH * 128], BF16)
            wvb = mk(stk, "wvb", [128, DCH * DH], BF16)
            bqfm = mk(stk, "bqfm", [64, S], BF16)
            bkfm = mk(stk, "bkfm", [64, S], BF16)
            bvfm = mk(stk, "bvfm", [64, S], BF16)
            kqfm = mk(stk, "kqfm", [64, NT2], BF16)
            sqf = mk(stk, "sqf", [128, 2 * TC], BF16)
            nrst = mk(stk, "nrst", [2, TC], BF16)
            qz = mk(stk, "qz", [64, 1], BF16)
            zqz = p.v(lambda e: e.memset(qz[:], 0.0))
            psp = mkp(stk, "psp", [128, 2 * TC])
            psn = mkp(stk, "psn", [2, 2 * TC])
            psc = mkp(stk, "psc", [1, 2 * TC])
            if "beta" in P1SUB:
                _beta_on = True
            else:
                _beta_on = False
            if _beta_on:
                lb = p.dma(wstage[:, 0:128].rearrange("q (c m) -> q c m", c=DCH),
                           wbbeta.rearrange("(c q) m -> q c m", q=128), waits=[t_ln])
                cvb = p.s(lambda e: e.copy(wqkb[:, 0:128], wstage[:, 0:128]), waits=[lb])
                for tc in range(NTC):
                    for c in range(DCH):
                        p.t(lambda e, tc=tc, c=c: e.matmul(
                            psp[0:16, 0:TC],
                            wqkb[:, 0:128].rearrange("q (c n) -> q c n", c=DCH)[:, c, :],
                            xh(c)[:, TC * tc:TC * tc + TC],
                            start=(c == 0), stop=(c == DCH - 1)),
                            waits=[cvb, t_ln, p.now("act")] + ([p.now("dmg")] if tc >= 1 else []))
                    tm = p.now("pe")
                    sg = p.s(lambda e: e.activation(sqf[0:16, 0:TC], psp[0:16, 0:TC], ACT.Sigmoid),
                             waits=[tm, p.now("dmg")])
                    p.dma(s_nr[:, 2, TC * tc:TC * tc + TC], sqf[0:16, 0:TC], waits=[sg], q="gpsimd")
            if "heads" not in P1SUB:
                return p.barrier()
            for h in range(H):
                l1 = p.dma(wstage[:].rearrange("q (c n) -> q c n", c=2 * DCH)[:, 0:DCH, 0:64],
                           wbq.rearrange("(c q) m -> q c m", q=128)[:, :, 64 * h:64 * h + 64],
                           waits=[p.now("dve"), p.now("act")])
                l2 = p.dma(wstage[:].rearrange("q (c n) -> q c n", c=2 * DCH)[:, 0:DCH, 64:128],
                           wbk.rearrange("(c q) m -> q c m", q=128)[:, :, 64 * h:64 * h + 64])
                l3 = p.dma(wstage[:].rearrange("q (c n) -> q c n", c=2 * DCH)[:, DCH:2 * DCH, 0:64],
                           wbv.rearrange("(c q) m -> q c m", q=128)[:, :, 64 * h:64 * h + 64])
                cv1 = p.s(lambda e: e.copy(wqkb[:], wstage[:, 0:1024]), waits=[l1, l2, p.now("pe")])
                cv2 = p.s(lambda e: e.copy(wvb[:], wstage[:].rearrange(
                    "q (c n) -> q c n", c=2 * DCH)[:, DCH:2 * DCH, 0:64]), waits=[l3])
                for tc in range(NTC):
                    pp = tc % 2
                    for c in range(DCH):
                        p.t(lambda e, tc=tc, c=c, pp=pp: e.matmul(
                            psp[:, TC * pp:TC * pp + TC],
                            wqkb[:].rearrange("q (c n) -> q c n", c=DCH)[:, c, :],
                            xh(c)[:, TC * tc:TC * tc + TC],
                            start=(c == 0), stop=(c == DCH - 1)),
                            waits=[cv1, t_ln, p.now("act")])
                    tm = p.now("pe")
                    wov = [p.now("dve"), p.now("dms")] if tc == 0 else []
                    p.s(lambda e, tc=tc, pp=pp: e.copy(bqfm[:, TC * tc:TC * tc + TC],
                                                       psp[0:64, TC * pp:TC * pp + TC]), waits=[tm] + wov)
                    p.s(lambda e, tc=tc, pp=pp: e.copy(bkfm[:, TC * tc:TC * tc + TC],
                                                       psp[64:128, TC * pp:TC * pp + TC]), waits=[tm])
                    if "norms" not in HEADSUB:
                        continue
                    sq = p.s(lambda e, pp=pp: e.activation(sqf[:, 0:TC], psp[:, TC * pp:TC * pp + TC], ACT.Square),
                             waits=([p.now("dms"), p.now("dmg")] if tc == 0 else []))
                    nm = p.t(lambda e, pp=pp: e.matmul(psn[0:2, 0:TC], onesb[:, 1:3], sqf[:, 0:TC],
                                                       start=True, stop=True),
                             waits=[sq, p.now("dmg")])
                    sr = p.s(lambda e: e.activation(smalls[0:2, 0:TC], psn[0:2, 0:TC], ACT.Sqrt),
                             waits=[nm, p.now("dve")])
                    vmx = p.v(lambda e: e.tensor_scalar(smalls[0:2, 0:TC], smalls[0:2, 0:TC], 1e-12, None, ALU.max),
                              waits=[sr])
                    vrc = p.v(lambda e: e.reciprocal(smalls[0:2, TC:2 * TC], smalls[0:2, 0:TC]))
                    vcp = p.v(lambda e: e.tensor_copy(nrst[0:2, 0:TC], smalls[0:2, TC:2 * TC]),
                              waits=[p.now("dmg")])
                    p.dma(s_nr[h, 0:2, TC * tc:TC * tc + TC], nrst[0:2, 0:TC], waits=[vcp], q="gpsimd")
                t_qk = p.now("act")
                for tc in range(NTC):
                    pp = tc % 2
                    for c in range(DCH):
                        p.t(lambda e, tc=tc, c=c, pp=pp: e.matmul(
                            psp[0:64, TC * pp:TC * pp + TC],
                            wvb[:].rearrange("q (c n) -> q c n", c=DCH)[:, c, :],
                            xh(c)[:, TC * tc:TC * tc + TC],
                            start=(c == 0), stop=(c == DCH - 1)),
                            waits=[cv2, p.now("act")])
                    tm = p.now("pe")
                    p.s(lambda e, tc=tc, pp=pp: e.copy(bvfm[:, TC * tc:TC * tc + TC],
                                                       psp[0:64, TC * pp:TC * pp + TC]),
                        waits=[tm] + ([p.now("dms")] if tc == 0 else []))
                t_v = p.now("act")
                if "kq" not in HEADSUB:
                    continue
                kq = p.v(lambda e: e.tensor_tensor(kqfm[:], bkfm[:, 0:S:2], bqfm[:, 0:S:2], ALU.mult),
                         waits=[t_qk, p.now("dms"), p.now("pe")])
                cm1 = p.t(lambda e: e.matmul(psc[0:1, 0:TC], onesb[0:64, 0:1], kqfm[:, 0:TC],
                                             start=True, stop=True), waits=[kq, p.now("act")])
                cm2 = p.t(lambda e: e.matmul(psc[0:1, TC:2 * TC], onesb[0:64, 0:1], kqfm[:, TC:2 * TC],
                                             start=True, stop=True))
                ce = p.s(lambda e: e.copy(sqf[0:1, 0:TC], psc[0:1, 0:TC]), waits=[cm1, p.now("dms")])
                ce2 = p.s(lambda e: e.copy(sqf[0:1, TC:2 * TC], psc[0:1, TC:2 * TC]), waits=[cm2])
                p.dma(s_cd[h].rearrange("(o n) -> o n", o=1), sqf[0:1, 0:2 * TC], waits=[ce, ce2], q="gpsimd")
                if "stage" not in HEADSUB:
                    continue
                d1 = p.dma(s_qfm[h], bqfm[:], waits=[t_qk], q="scalar")
                d2 = p.dma(s_kfm[h], bkfm[:], waits=[t_qk], q="scalar")
                d3 = p.dma(s_vfm[h], bvfm[:], waits=[t_v], q="scalar")
                h2, g = h % 2, h // 2
                c1 = p.v(lambda e: e.tensor_copy(sqf.bitcast(F32)[:, 0:TC].bitcast(BF16)[0:64, 0:NT2]
                                                 if False else sqf[0:64, 0:NT2], bqfm[:, 1:S:2]),
                         waits=[t_qk, p.now("dms"), p.now("dmg")])
                c2 = p.v(lambda e: e.tensor_copy(sqf[64:128, 0:NT2 - 1], bqfm[:, 2:S:2]))
                c3 = p.v(lambda e: e.memset(sqf[64:128, NT2 - 1:NT2], 0.0))
                p.dma(s_qv[0, h], sqf[0:64, 0:NT2], waits=[c1], q="scalar")
                p.dma(s_qv[1, h], sqf[64:128, 0:NT2], waits=[c3], q="scalar")
                p.dma(s_qv[2, h, :, 0:NT2 - 1], kqfm[:, 1:NT2], waits=[kq], q="scalar")
                p.dma(s_qv[2, h, :, NT2 - 1:NT2], qz[:], waits=[zqz], q="scalar", allow_slow_non_contiguous=True)

            if "gate" not in P1SUB:
                return p.barrier()
            for mc in range(DCH):
                cv = None
                for fc in range(DCH):
                    pp = fc % 2
                    l = p.dma(wstage[:, 128 * pp:128 * pp + 128],
                              wbg[128 * fc:128 * fc + 128, 128 * mc:128 * mc + 128],
                              waits=[p.now("act")])
                    cv = p.s(lambda e, fc=fc, pp=pp: e.copy(
                        wqkb[:].rearrange("q (c n) -> q c n", c=DCH)[:, fc, :],
                        wstage[:, 128 * pp:128 * pp + 128]),
                        waits=[l] + ([p.now("pe")] if fc == 0 else []))
                for tc in range(NTC):
                    pp = tc % 2
                    for fc in range(DCH):
                        p.t(lambda e, tc=tc, fc=fc, pp=pp: e.matmul(
                            psp[:, TC * pp:TC * pp + TC],
                            wqkb[:].rearrange("q (c n) -> q c n", c=DCH)[:, fc, :],
                            xh(fc)[:, TC * tc:TC * tc + TC],
                            start=(fc == 0), stop=(fc == DCH - 1)),
                            waits=[cv, t_ln, p.now("act"), p.now("dms")])
                    tm = p.now("pe")
                    sg = p.s(lambda e, tc=tc, pp=pp: e.activation(sqf[:, 0:TC], psp[:, TC * pp:TC * pp + TC], ACT.Silu),
                             waits=[tm])
                    p.dma(s_gate[mc, :, TC * tc:TC * tc + TC], sqf[:, 0:TC], waits=[sg], q="scalar")
            if "chunks" not in P1SUB:
                return p.barrier()
            pr = {}
            for nm_ in ("tkE", "tkO", "tvE", "tvO", "khe", "kho", "kheo", "nkbe", "nkbo", "kbeo", "vbe", "vbo", "vbek"):
                pr[nm_] = mk(stk, "pr" + nm_, [TBC, 16 * DH], BF16)
            stg = mk(stk, "stg", [64, 256], BF16)
            srow = mk(stk, "srow", [16, 256], BF16)
            stok = mk(stk, "stok", [TBC, 8 * 16], BF16)
            pst = mkp(stk, "pst", [128, 2 * 128], BF16)
            for cb in range(8):
                tb0 = TBC * cb
                for si, (ridx, par) in enumerate(((1, 0), (1, 1), (2, 0), (2, 1))):
                    ldr = p.dma(srow[:, 0:TBC], s_nr[:, ridx, 2 * tb0 + par:2 * tb0 + 2 * TBC:2],
                                waits=[p.now("pe"), p.now("act"), p.now("dve"), p.now("dmg")],
                                q="sync", allow_slow_non_contiguous=True)
                    tt = p.t(lambda e, si=si: e.transpose(pst[0:TBC, 128 * (si % 2):128 * (si % 2) + 16],
                                                          srow[:, 0:TBC], idt[0:16, 0:16]),
                             waits=[ldr, p.now("act")])
                    p.s(lambda e, si=si: e.copy(stok[:, 16 * si:16 * si + 16],
                                                pst[0:TBC, 128 * (si % 2):128 * (si % 2) + 16]), waits=[tt])
                t_sig = p.now("act")
                nb = p.v(lambda e: e.tensor_scalar(stok[:, 64:96], stok[:, 32:64], -1.0, None, ALU.mult),
                         waits=[t_sig])
                tprev = p.now("act")
                for h in range(H):
                    l1 = p.dma(stg[:, 0:256], s_kfm[h, :, 2 * tb0:2 * tb0 + 2 * TBC],
                               waits=[p.now("pe"), p.now("dms")], q="gpsimd")
                    for par, dst in ((0, "tkE"), (1, "tkO")):
                        tt = p.t(lambda e, par=par: e.transpose(pst[0:TBC, 128 * par:128 * par + 64],
                                                                stg[:, par:256:2], idt[0:64, 0:64]),
                                 waits=[l1, p.now("act")])
                        p.s(lambda e, h=h, dst=dst, par=par: e.copy(
                            pr[dst][:, 64 * h:64 * h + 64], pst[0:TBC, 128 * par:128 * par + 64]), waits=[tt])
                    l2 = p.dma(stg[:, 0:256], s_vfm[h, :, 2 * tb0:2 * tb0 + 2 * TBC],
                               waits=[p.now("pe"), p.now("act")], q="gpsimd")
                    for par, dst in ((0, "tvE"), (1, "tvO")):
                        tt = p.t(lambda e, par=par: e.transpose(pst[0:TBC, 128 * par:128 * par + 64],
                                                                stg[:, par:256:2], idt[0:64, 0:64]),
                                 waits=[l2, p.now("act")])
                        p.s(lambda e, h=h, dst=dst, par=par: e.copy(
                            pr[dst][:, 64 * h:64 * h + 64], pst[0:TBC, 128 * par:128 * par + 64]), waits=[tt])
                t_tok = p.now("act")
                def sB(col):
                    return stok[:, 16 * col:16 * col + 16].rearrange("t h -> t h").unsqueeze(2).broadcast_to([TBC, 16, DH])

                def pv3(t):
                    return pr[t][:].rearrange("t (h i) -> t h i", h=16)

                v1 = p.v(lambda e: e.tensor_tensor(pv3("khe"), pv3("tkE"), sB(0), ALU.mult), waits=[t_tok, nb])
                p.v(lambda e: e.tensor_tensor(pv3("kho"), pv3("tkO"), sB(1), ALU.mult))
                p.v(lambda e: e.tensor_tensor(pv3("kheo"), pv3("khe"), pv3("kho"), ALU.mult))
                p.v(lambda e: e.tensor_tensor(pv3("nkbe"), pv3("khe"), sB(4), ALU.mult))
                p.v(lambda e: e.tensor_tensor(pv3("nkbo"), pv3("kho"), sB(5), ALU.mult))
                p.v(lambda e: e.tensor_tensor(pv3("kbeo"), pv3("nkbe"), pv3("nkbo"), ALU.mult))
                p.v(lambda e: e.tensor_tensor(pv3("vbe"), pv3("tvE"), sB(2), ALU.mult))
                p.v(lambda e: e.tensor_tensor(pv3("vbo"), pv3("tvO"), sB(3), ALU.mult))
                tl = p.v(lambda e: e.tensor_tensor(pv3("vbek"), pv3("vbe"), pv3("nkbo"), ALU.mult))
                for h2p in range(2):
                    for r, t in ((0, "khe"), (1, "kho"), (2, "kheo")):
                        p.dma(s_kp[h2p, :, r, tb0:tb0 + TBC, :].rearrange("g t j -> t g j"),
                              pr[t][:].rearrange("t (h j) -> t h j", h=16)[:, h2p:16:2, :],
                              waits=[tl], q="scalar")
                    for r, t in ((0, "nkbe"), (1, "nkbo"), (2, "kbeo")):
                        p.dma(s_ipa[:, h2p, r, tb0:tb0 + TBC, :].rearrange("g t j -> t g j"),
                              pr[t][:].rearrange("t (h j) -> t h j", h=16)[:, h2p:16:2, :],
                              waits=[tl], q="scalar")
                    for r, t in ((0, "vbe"), (1, "vbo"), (2, "vbek")):
                        p.dma(s_ipb[:, h2p, r, tb0:tb0 + TBC, :].rearrange("g t j -> t g j"),
                              pr[t][:].rearrange("t (h j) -> t h j", h=16)[:, h2p:16:2, :],
                              waits=[tl], q="scalar")
            return p.barrier()

    def attn_b_p2(seq, wait0):
        with contextlib.ExitStack() as stk:
            laA = mk(stk, "laA", [64, 2 * TSP * 128], BF16)
            laB = mk(stk, "laB", [64, 2 * TSP * 128], BF16)
            raA = xhat[0:64, :]
            raB = mk(stk, "raB", [64, 2 * TSP * 512], BF16)
            qvr = mk(stk, "qvr", [128, 2 * TSP * 48], BF16)
            qstage = mk(stk, "qstage", [64, 2 * TSP * 8], BF16)
            Sring = mk(stk, "Sring", [128, 9 * 512], F32)
            mtmp = mk(stk, "mtmp", [128, 512], F32)
            rawc = mk(stk, "rawc", [64, 4 * 512], BF16)
            onerow = mk(stk, "onerow", [1, 128], BF16)
            psa = mkp(stk, "psa", [128, 2 * 512])
            psb2 = mkp(stk, "psb2", [128, 2 * 512])
            pct = mkp(stk, "pct", [128, 4 * 512])
            z1 = p.v(lambda e: e.memset(laA[:], 0.0), waits=list(wait0))
            z2 = p.v(lambda e: e.memset(laB[:], 0.0))
            z3 = p.v(lambda e: e.memset(raA[:], 0.0))
            z4 = p.v(lambda e: e.memset(raB[:], 0.0))
            z5 = p.v(lambda e: e.memset(qvr[:], 0.0))
            z6 = p.v(lambda e: e.memset(Sring[:, 8 * 512:9 * 512], 0.0))
            z7 = p.v(lambda e: e.memset(onerow[:], 1.0))
            zt = [z1, z2, z3, z4, z5, z6, z7]
            d0 = p.dma(s_misc[0:1, 0:32], onerow[0:1, 0:64].bitcast(F32)[0:1, 0:32], waits=[z7], q="gpsimd")
            ones_src = bass.AP(s_misc.tensor, 0, [[0, 8], [0, 2 * TSP], [1, 32]])
            od = []
            for h2p in range(2):
                od.append(p.dma(laA[h2p:64:8, :].rearrange("p (t j) -> p t j", t=2 * TSP)
                                [:, :, 64 * h2p:64 * h2p + 64].bitcast(F32) if False else
                                laA[h2p:64:8, :].bitcast(F32).rearrange("p (t j) -> p t j", t=2 * TSP)
                                [:, :, 32 * h2p:32 * h2p + 32],
                                ones_src, waits=[z1, d0], q="gpsimd"))
                for gp in range(8):
                    od.append(p.dma(raA[8 * gp + h2p:8 * gp + h2p + 1, :].bitcast(F32)
                                    .rearrange("p (t i) -> p t i", t=2 * TSP)[:, :, 32 * gp:32 * gp + 32],
                                    bass.AP(s_misc.tensor, 0, [[0, 1], [0, 2 * TSP], [1, 32]]),
                                    waits=[z3, d0], q="gpsimd"))
            filltok = {}
            steptok = {}
            ctok = {}
            evtok = {}

            def emit_fills(sp, gate=None):
                half = sp % 2
                t0 = sp * TSP
                w = gate if gate is not None else (zt + od)
                toks = []
                for h2p in range(2):
                    for r in range(3):
                        toks.append(p.dma(
                            laA[:].rearrange("p (b t j) -> p b t j", b=2, t=TSP)
                            [2 * (r + 1) + h2p:64:8, half, :, 64 * h2p:64 * h2p + 64],
                            s_kp[h2p, :, r, t0:t0 + TSP, :],
                            waits=w, q="gpsimd"))
                        toks.append(p.dma(
                            laB[:].rearrange("p (b t j) -> p b t j", b=2, t=TSP)
                            [2 * r + h2p:48:6, half, :, 64 * h2p:64 * h2p + 64],
                            s_kp[h2p, :, r, t0:t0 + TSP, :],
                            waits=w, q="gpsimd"))
                for gp in range(8):
                    for h2p in range(2):
                        toks.append(p.dma(
                            raA[:].rearrange("p (b t i) -> p b t i", b=2, t=TSP)
                            [8 * gp + 2 + h2p:8 * gp + 8:2, half, :, 64 * gp:64 * gp + 64],
                            s_ipa[gp, h2p, :, t0:t0 + TSP, :],
                            waits=w, q="gpsimd"))
                        toks.append(p.dma(
                            raB[:].rearrange("p (b t i) -> p b t i", b=2, t=TSP)
                            [6 * gp + h2p:6 * gp + 6:2, half, :, 64 * gp:64 * gp + 64],
                            s_ipb[gp, h2p, :, t0:t0 + TSP, :],
                            waits=w, q="gpsimd"))
                for h2p in range(2):
                    for v in range(3):
                        sslot = (2 * (half * 3 + v) + h2p) % 2
                        ld = p.dma(qstage[:, TSP * 8 * sslot:TSP * 8 * sslot + TSP * 8]
                                   .rearrange("j (g t) -> j g t", g=8),
                                   s_qv[v, h2p:16:2, :, t0:t0 + TSP].rearrange("g j t -> j g t"),
                                   waits=w + [p.now("dve")], q="gpsimd")
                        toks.append(p.v(lambda e, h2p=h2p, v=v, half=half, sslot=sslot: e.tensor_copy(
                            qvr[64 * h2p:64 * h2p + 64, :].rearrange("p (b t h2v v gp) -> p b t h2v v gp",
                                                                     b=2, t=TSP, h2v=2, v=3)
                            [:, half, :, h2p, v, :],
                            qstage[:, TSP * 8 * sslot:TSP * 8 * sslot + TSP * 8]
                            .rearrange("j (g t) -> j t g", g=8), waits=[ld]) if False else
                            e.tensor_copy(
                            qvr[64 * h2p:64 * h2p + 64, :].rearrange("p (b t h2v v gp) -> p b t h2v v gp",
                                                                     b=2, t=TSP, h2v=2, v=3)
                            [:, half, :, h2p, v, :],
                            qstage[:, TSP * 8 * sslot:TSP * 8 * sslot + TSP * 8]
                            .rearrange("j (g t) -> j t g", g=8)), waits=[ld]))
                filltok[sp] = toks

            emit_fills(0)
            emit_fills(1)
            for tt in range(NT2):
                sp = tt // TSP
                half = sp % 2
                ti = tt % TSP
                wfill = filltok[sp]
                wa = wfill + ([steptok[tt - 2]] if tt >= 2 else [])
                a2 = p.t(lambda e, half=half, ti=ti, tt=tt: e.matmul(
                    psa[:, 512 * (tt % 2):512 * (tt % 2) + 512],
                    laA[:].rearrange("p (b t j) -> p b t j", b=2, t=TSP)[:, half, ti, :],
                    raA[:].rearrange("p (b t i) -> p b t i", b=2, t=TSP)[:, half, ti, :],
                    start=True, stop=True), waits=wa)
                b2 = p.t(lambda e, half=half, ti=ti, tt=tt: e.matmul(
                    psb2[:, 512 * (tt % 2):512 * (tt % 2) + 512],
                    laB[:].rearrange("p (b t j) -> p b t j", b=2, t=TSP)[0:48, half, ti, :],
                    raB[:].rearrange("p (b t i) -> p b t i", b=2, t=TSP)[0:48, half, ti, :],
                    start=True, stop=True))
                sprev = Sring[:, 512 * ((tt - 1) % 8):512 * ((tt - 1) % 8) + 512] if tt > 0 else Sring[:, 8 * 512:9 * 512]
                scur = Sring[:, 512 * (tt % 8):512 * (tt % 8) + 512]
                wv = [b2] + ([ctok[tt - 8]] if tt >= 8 else [z6])
                p.v(lambda e, sprev=sprev, tt=tt: e.tensor_tensor(
                    mtmp[:], psa[:, 512 * (tt % 2):512 * (tt % 2) + 512], sprev, ALU.mult), waits=wv)
                st = p.v(lambda e, scur=scur, tt=tt: e.tensor_tensor(
                    scur, mtmp[:], psb2[:, 512 * (tt % 2):512 * (tt % 2) + 512], ALU.add))
                steptok[tt] = st
                wc = [st] + ([evtok[tt - 4]] if tt >= 4 else [])
                ct = p.t(lambda e, half=half, ti=ti, tt=tt, scur=scur: e.matmul(
                    pct[0:48, 512 * (tt % 4):512 * (tt % 4) + 512],
                    qvr[:].rearrange("p (b t v) -> p b t v", b=2, t=TSP)[:, half, ti, :],
                    scur.bitcast(BF16)[:, 1:1024:2],
                    start=True, stop=True), waits=wc)
                ctok[tt] = ct
                ev = p.s(lambda e, tt=tt: e.copy(rawc[0:48, 512 * (tt % 4):512 * (tt % 4) + 512],
                                                 pct[0:48, 512 * (tt % 4):512 * (tt % 4) + 512]),
                         waits=[ct] + ([evtok[tt - 4]] if tt >= 4 else []))
                dmp = p.dma(s_r[tt], rawc[0:48, 512 * (tt % 4):512 * (tt % 4) + 512],
                            waits=[ev], q="scalar")
                evtok[tt] = dmp
                if ti == TSP - 1 and sp + 2 <= (NT2 // TSP) - 1:
                    emit_fills(sp + 2, [ct])
            return p.barrier()

    def attn_b_p3(seq, wait0):
        with contextlib.ExitStack() as stk:
            tok_r = mk(stk, "tok_r", [TBC, 3 * 1024], BF16)
            tokout = mk(stk, "tokout", [TBC, 2 * 1024], BF16)
            nkbe = mk(stk, "nkbe", [TBC, 1024], BF16)
            vbe = mk(stk, "vbe", [TBC, 1024], BF16)
            sctok = mk(stk, "sctok", [TBC, 10 * 16], BF16)
            ctmp = mk(stk, "ctmp", [TBC, 1024], F32)
            ctm2 = mk(stk, "ctm2", [TBC, 1024], BF16)
            lncol = mk(stk, "lncol", [TBC, 4], F32)
            lnvr = mk(stk, "lnvr", [TBC, 16], F32)[:, 0:1]
            lnsd = mk(stk, "lnsd", [TBC, 16], F32)[:, 0:1]
            lntmp = qtmp[:, 0:1024]
            srow = mk(stk, "srow", [16, 256], BF16)
            outfm = mk(stk, "outfm", [128, DCH * TC], BF16)
            gfm = mk(stk, "gfm", [128, DCH * TC], BF16)
            wob_all = mk(stk, "wob_all", [128, DCH * DCH * 128], BF16)
            wstage = mk(stk, "wstage", [128, 2 * 128], F32)
            pst = mkp(stk, "pst", [128, 2 * 128], BF16)
            psw = mkp(stk, "psw", [128, 2 * TC])

            def t3(ap):
                return ap.rearrange("t (h i) -> t h i", h=16)

            def sB(col):
                return sctok[:, 16 * col:16 * col + 16].unsqueeze(2).broadcast_to([TBC, 16, DH])

            cvw = None
            for mc in range(DCH):
                for fc in range(DCH):
                    pp = fc % 2
                    l = p.dma(wstage[:, 128 * pp:128 * pp + 128],
                              wbout[128 * fc:128 * fc + 128, 128 * mc:128 * mc + 128],
                              waits=[p.now("act")] if (fc >= 2 or mc > 0) else list(wait0))
                    cvw = p.s(lambda e, mc=mc, fc=fc, pp=pp: e.copy(
                        wob_all[:].rearrange("q (m c n) -> q m c n", m=DCH, c=DCH)[:, mc, fc, :],
                        wstage[:, 128 * pp:128 * pp + 128]), waits=[l])
            t_wout = p.now("act")

            for cb in range(P3CB):
                tb0 = TBC * cb
                wpe = [p.now("pe"), p.now("act"), p.now("dve")] + list(wait0)
                if ATTNB_DBG:
                    wpe.append(p.now("dms"))
                lr = []
                for v in (range(3) if "rawload" in P3SUB else []):
                    for h2p in range(2):
                        if v == 0:
                            nrow = TBC
                            off = tb0 * 48 * 512 + h2p * 12288 + v * 4096
                        else:
                            nrow = TBC if cb > 0 else TBC - 1
                            off = ((tb0 - 1) if cb > 0 else 0) * 48 * 512 + h2p * 12288 + v * 4096
                        dst = tok_r[TBC - nrow:TBC, 1024 * v + 512 * h2p:1024 * v + 512 * h2p + 512]
                        lr.append(p.dma(dst.rearrange("t (g i) -> t g i", g=8),
                                        bass.AP(s_r.tensor, off,
                                                [[48 * 512, nrow], [576, 8], [1, 64]]),
                                        waits=wpe, q="sync"))
                if cb == 0:
                    lr.append(p.v(lambda e: e.memset(tok_r[0:1, 1024:3072], 0.0), waits=wpe))
                la = lb = None
                for h2p in (range(2) if "ipload" in P3SUB else []):
                    la = p.dma(nkbe[:, 512 * h2p:512 * h2p + 512].rearrange("t (g i) -> t g i", g=8),
                               s_ipa[:, h2p, 0, tb0:tb0 + TBC, :].rearrange("g t i -> t g i"),
                               waits=wpe, q="sync")
                    lb = p.dma(vbe[:, 512 * h2p:512 * h2p + 512].rearrange("t (g i) -> t g i", g=8),
                               s_ipb[:, h2p, 0, tb0:tb0 + TBC, :].rearrange("g t i -> t g i"),
                               waits=wpe, q="sync")
                sig = []
                for si, (ridx, par) in enumerate((((0, 0), (0, 1), (1, 0))) if "sig" in P3SUB else ()):
                    ldr = p.dma(srow[:, 0:TBC], s_nr[:, ridx, 2 * tb0 + par:2 * tb0 + 2 * TBC:2],
                                waits=wpe + [p.now("pe")], q="sync", allow_slow_non_contiguous=True)
                    tt = p.t(lambda e, si=si: e.transpose(pst[0:TBC, 128 * (si % 2):128 * (si % 2) + 16],
                                                          srow[:, 0:TBC], idt[0:16, 0:16]),
                             waits=[ldr, p.now("act")])
                    sig.append(p.s(lambda e, si=si: e.copy(
                        sctok[:, 64 + 16 * si:64 + 16 * si + 16].rearrange("t (h2 g) -> t h2 g", h2=2),
                        pst[0:TBC, 128 * (si % 2):128 * (si % 2) + 16].rearrange("t (g h2) -> t h2 g", g=8)),
                        waits=[tt]))
                if "sig" not in P3SUB:
                    continue
                ldc = p.dma(srow[:, 0:TBC], s_cd[:, tb0:tb0 + TBC], waits=wpe + [p.now("pe")], q="sync")
                ttc = p.t(lambda e: e.transpose(pst[0:TBC, 128:144], srow[:, 0:TBC], idt[0:16, 0:16]), waits=[ldc, p.now("act")])
                sig.append(p.s(lambda e: e.copy(
                    sctok[:, 112:128].rearrange("t (h2 g) -> t h2 g", h2=2),
                    pst[0:TBC, 128:144].rearrange("t (g h2) -> t h2 g", g=8)), waits=[ttc]))
                s2 = p.v(lambda e: e.tensor_tensor(sctok[:, 128:144], sctok[:, 64:80], sctok[:, 96:112], ALU.mult),
                         waits=list(sig))
                cs = p.v(lambda e: e.tensor_tensor(sctok[:, 144:160], sctok[:, 112:128], sctok[:, 128:144], ALU.mult),
                         waits=[s2])

                def sBr(c0):
                    return sctok[:, c0:c0 + 16].rearrange("t h -> t h").unsqueeze(2).broadcast_to([TBC, 16, DH])

                if "combine" not in P3SUB:
                    continue
                if ATTNB_DBG:
                    lr.append(p.dma(dbg_tkr[cb], tok_r[:], waits=lr + sig, q="scalar"))
                    cs = p.dma(dbg_sct[cb], sctok[:], waits=[cs], q="scalar")
                o1 = p.v(lambda e: e.tensor_tensor(t3(ctmp[:]), t3(tok_r[:, 1024:2048]), sBr(64), ALU.mult),
                         waits=lr + [s2, cs] + ([la, lb] if la else []))
                o2 = p.v(lambda e: e.tensor_tensor(t3(ctm2[:]), t3(tok_r[:, 2048:3072]), sBr(128), ALU.mult))
                o3 = p.v(lambda e: e.tensor_tensor(t3(ctm2[:]), t3(ctm2[:]), t3(nkbe[:]), ALU.mult))
                o4 = p.v(lambda e: e.tensor_tensor(t3(ctmp[:]), t3(ctmp[:]), t3(ctm2[:]), ALU.add))
                o5 = p.v(lambda e: e.tensor_tensor(t3(ctm2[:]), t3(vbe[:]), sBr(144), ALU.mult))
                o6 = p.v(lambda e: e.tensor_tensor(t3(tokout[:, 0:1024]), t3(ctmp[:]), t3(ctm2[:]), ALU.add))
                o7 = p.v(lambda e: e.tensor_tensor(t3(tokout[:, 1024:2048]), t3(tok_r[:, 0:1024]), sBr(80), ALU.mult))
                if ATTNB_DBG:
                    dtk = p.dma(dbg_tok[cb], tokout[:], waits=[o6, o7], q="scalar")
                    o6 = o7 = dtk
                if "ln" not in P3SUB:
                    continue
                lnw = [o6, o7]
                for par in range(2):
                    sl = tokout[:, 1024 * par:1024 * par + 1024]
                    a1 = p.s(lambda e, sl=sl: e.activation(ctmp[:], sl, ACT.Identity,
                                                           scale=1.0 / 1024.0, accum_out=lncol[:, 0:1]),
                             waits=lnw)
                    n1 = p.v(lambda e, sl=sl: e.tensor_tensor(
                        lntmp[:], sl, lncol[:, 0:1].broadcast_to([TBC, 1024]), ALU.subtract), waits=[a1])
                    a2 = p.s(lambda e: e.activation(ctmp[:], lntmp[:], ACT.Square, accum_out=lnvr[:]),
                             waits=[n1])
                    a3 = p.s(lambda e: e.activation(lnsd[:], lnvr[:], ACT.Ln,
                                                    scale=1.0 / 1024.0, bias=1e-5))
                    a4 = p.s(lambda e: e.activation(lnsd[:], lnsd[:], ACT.Exp, scale=-0.5))
                    n2 = p.v(lambda e: e.tensor_tensor(
                        lntmp[:], lntmp[:], lnsd[:].broadcast_to([TBC, 1024]), ALU.mult), waits=[a4])
                    n3 = p.v(lambda e, sl=sl: e.tensor_tensor(lntmp[:], lntmp[:], gamB[0:TBC, :], ALU.mult))
                    n4 = p.v(lambda e, sl=sl: e.tensor_tensor(sl, lntmp[:], betB[0:TBC, :], ALU.add))
                    v6 = n4
                    lnw = [n4]
                    if ATTNB_DBG:
                        c1d = p.v(lambda e: e.tensor_copy(lncol[:, 2:3], lnsd[:]), waits=[v6])
                        c2d = p.v(lambda e: e.tensor_copy(lncol[:, 3:4], lnvr[:]))
                        lnw.append(p.dma(dbg_lnc[cb, :, 4 * par:4 * par + 4], lncol[:], waits=[c1d, c2d], q="scalar"))
                if ATTNB_DBG:
                    p.dma(dbg_tk2[cb], tokout[:], waits=[lnw[0]], q="scalar")
                tcmb = p.now("dve")
                if "tr" not in P3SUB:
                    continue
                for par in range(2):
                    for fs in range(DCH):
                        tt = p.t(lambda e, par=par, fs=fs: e.transpose(
                            pst[:, 128 * ((par * DCH + fs) % 2):128 * ((par * DCH + fs) % 2) + 128],
                            tokout[0:TBC, 1024 * par + 128 * fs:1024 * par + 128 * fs + 128], idt[:]),
                            waits=[tcmb, p.now("act")])
                        p.s(lambda e, par=par, fs=fs, cb=cb: e.copy(
                            outfm[:].rearrange("q (c n) -> q c n", c=DCH)
                            [:, fs, 256 * (cb % 2) + par:256 * (cb % 2) + 256:2],
                            pst[:, 128 * ((par * DCH + fs) % 2):128 * ((par * DCH + fs) % 2) + 128]),
                            waits=[tt])
                if cb % 2 == 1 and "wout" in P3SUB:
                    tg = cb // 2
                    t_tr = p.now("act")
                    lg = p.dma(gfm[:].rearrange("q (c n) -> q c n", c=DCH),
                               s_gate[:, :, TC * tg:TC * tg + TC].rearrange("c q n -> q c n"),
                               waits=[p.now("dve")], q="sync")
                    gm = []
                    for fs in range(DCH):
                        gm.append(p.v(lambda e, fs=fs: e.tensor_tensor(
                            outfm[:].rearrange("q (c n) -> q c n", c=DCH)[:, fs, :],
                            outfm[:].rearrange("q (c n) -> q c n", c=DCH)[:, fs, :],
                            gfm[:].rearrange("q (c n) -> q c n", c=DCH)[:, fs, :], ALU.mult),
                            waits=[lg, t_tr]))
                    if ATTNB_DBG:
                        gm.append(p.dma(dbg_ofm[tg], outfm[:], waits=list(gm), q="scalar"))
                    for mc in range(DCH):
                        pp = mc % 2
                        for fc in range(DCH):
                            p.t(lambda e, mc=mc, fc=fc, pp=pp: e.matmul(
                                psw[:, TC * pp:TC * pp + TC],
                                wob_all[:].rearrange("q (m c n) -> q m c n", m=DCH, c=DCH)[:, mc, fc, :],
                                outfm[:].rearrange("q (c n) -> q c n", c=DCH)[:, fc, :],
                                start=(fc == 0), stop=(fc == DCH - 1)),
                                waits=[t_wout, gm[-1]] + ([p.now("dve")] if mc >= 2 else []))
                        tm = p.now("pe")
                        p.v(lambda e, mc=mc, tg=tg, pp=pp: e.tensor_tensor(
                            xr(mc)[:, TC * tg:TC * tg + TC],
                            psw[:, TC * pp:TC * pp + TC],
                            xr(mc)[:, TC * tg:TC * tg + TC], ALU.add), waits=[tm])
            return p.barrier()

    def attn_b(seq, wait0):
        b1 = attn_b_p1(seq, wait0) if "p1" in ATTNB_PARTS else wait0
        b2 = attn_b_p2(seq, b1) if "p2" in ATTNB_PARTS else b1
        b3 = attn_b_p3(seq, b2) if "p3" in ATTNB_PARTS else b2
        return b3



    def load_x(seq, prev_store):
        toks = []
        for c in range(DCH):
            toks.append(p.dma(xr(c), x_in[seq].rearrange("(c q) s -> q c s", q=128)[:, c, :],
                              waits=prev_store))
        return toks

    def store_x(seq, barrier):
        toks = []
        for c in range(DCH):
            toks.append(p.dma(x_out[seq].rearrange("(c q) s -> q c s", q=128)[:, c, :], xr(c),
                              waits=barrier))
        return toks

    if "attn_b" in phases:
        with contextlib.ExitStack() as _gs:
            psg0 = mkp(_gs, "psg0", [128, 2 * TC])
            for half in range(2):
                for which, rowsrc, dstt in ((0, lngv, gamB), (1, lnbv, betB)):
                    zm = p.v(lambda e: e.memset(smalls[0:2, 0:TC], 0.0),
                             waits=[t_const, p.now("pe")])
                    cp = p.s(lambda e, rowsrc=rowsrc, half=half: e.copy(
                        smalls[0:1, 0:TC], rowsrc[0:1, TC * half:TC * half + TC]), waits=[zm])
                    bc = p.t(lambda e: e.matmul(psg0[:, 0:TC], sel2[:, 0:128], smalls[0:2, 0:TC],
                                                start=True, stop=True),
                             waits=[cp, t_sel, p.now("act")])
                    p.s(lambda e, dstt=dstt, half=half: e.copy(
                        dstt[:, TC * half:TC * half + TC], psg0[:, 0:TC]), waits=[bc])

    need_ffn = ("ffn1" in phases) or ("ffn2" in phases)
    prep_tok = None
    if need_ffn:
        prep_ffn_weights2()
        prep_tok = p.now("dma")
    prev_store = [("dma", 0)]
    for seq in range(NSEQ):
        lx = load_x(seq, prev_store)
        wait0 = lx + prev_store + [t_const]
        if seq == 0 and prep_tok is not None:
            wait0 = wait0 + [prep_tok, p.now("pe"), p.now("act"), p.now("dve"),
                             p.now("dma"), p.now("dmg"), p.now("dms")]
        bar = p.barrier() + wait0
        if "attn_a" in phases:
            bar = attn_a(seq, bar)
        if "ffn1" in phases:
            bar = ffn(seq, bar, True)
        if "attn_b" in phases:
            bar = attn_b(seq, bar)
        if "ffn2" in phases:
            bar = ffn(seq, bar, False)
        prev_store = store_x(seq, bar)
    p._op("sync", lambda e: None, [p.now("dma")], None, 0)

    with contextlib.ExitStack() as bs:
        sems = {s: bs.enter_context(nc.semaphore(s)) for s in ("dma", "dmg", "dms", "pe", "dve", "act", "gp")}
        block = bs.enter_context(nc.Block())

        def replay(eng_name):
            def run(eng):
                for e_, fn, waits, inc, amt in p.ops:
                    if e_ != eng_name:
                        continue
                    for s_, v_ in waits:
                        eng.wait_ge(sems[s_], v_)
                    ins = fn(eng)
                    if inc is not None and ins is not None:
                        ins.then_inc(sems[inc], amt)
            return run

        block.sync(replay("sync"))
        block.vector(replay("vector"))
        block.scalar(replay("scalar"))
        block.tensor(replay("tensor"))
        block.gpsimd(replay("gpsimd"))

    es_global.close()
    return nc


CONST_VALS_HOST = [1e-5, -14.0, 0.125, 1.0, 0.0, 1e-12, 127.0, 1.0 / 127.0, 1.0 / 1024.0, -0.5, 0, 0]

_FPERM = None


def _feat_perm():
    global _FPERM
    if _FPERM is None:
        idx = np.zeros(D, np.int64)
        for h2 in range(2):
            for g in range(8):
                for i in range(DH):
                    idx[512 * h2 + 64 * g + i] = (2 * g + h2) * 64 + i
        _FPERM = idx
    return _FPERM


def _permute_feat(a, axis):
    idx = _feat_perm()
    return np.take(a, idx, axis=axis)


def make_consts():
    import ml_dtypes
    bf = ml_dtypes.bfloat16
    c_id = np.eye(128, dtype=np.float32).astype(bf)
    mask = np.zeros((4, 128, TC), np.float32)
    for m in range(4):
        for u in range(128):
            mask[m, u, 0:min(TC, m * 128 + u)] = -1e30
    c_mask = mask.astype(bf)
    c_ones1 = np.ones((1, 128), np.float32)
    onesb = np.zeros((128, 3), np.float32)
    onesb[:, 0] = 1.0
    onesb[0:64, 1] = 1.0
    onesb[64:128, 2] = 1.0
    return {"c_id": np.ascontiguousarray(c_id), "c_mask": np.ascontiguousarray(c_mask),
            "c_ones1": c_ones1, "c_onesb": np.ascontiguousarray(onesb.astype(bf)),
            "c_vals": np.asarray([CONST_VALS_HOST], np.float32),
            "c_sel2": np.ascontiguousarray(np.stack([
                np.concatenate([np.ones(128, np.float32), np.zeros(128, np.float32)]),
                np.concatenate([np.zeros(128, np.float32), np.ones(128, np.float32)])]))}


def kernel(**inputs):
    x = np.ascontiguousarray(inputs["L0_attn_a_inputs"])
    consts = make_consts()
    base = {
        "wqkv": np.ascontiguousarray(inputs["L0_attn_a_W_qkv"]),
        "wouta": np.ascontiguousarray(inputs["L0_attn_a_W_out"]),
        "wbq": np.ascontiguousarray(inputs["L0_attn_b_Wq"]),
        "wbk": np.ascontiguousarray(inputs["L0_attn_b_Wk"]),
        "wbv": np.ascontiguousarray(inputs["L0_attn_b_Wv"]),
        "wbbeta": np.ascontiguousarray(inputs["L0_attn_b_Wbeta"]),
        "wbg": np.ascontiguousarray(_permute_feat(inputs["L0_attn_b_Wg"], axis=1)),
        "wbout": np.ascontiguousarray(_permute_feat(inputs["L0_attn_b_Wout"], axis=0)),
        "lng": np.ascontiguousarray(_permute_feat(inputs["L0_attn_b_ln_g"].reshape(1, D), axis=1)),
        "lnb": np.ascontiguousarray(_permute_feat(inputs["L0_attn_b_ln_b"].reshape(1, D), axis=1)),
        "w1": np.ascontiguousarray(inputs["L0_ffn_w1"]),
        "convw": np.ascontiguousarray(inputs["L0_ffn_conv_w"].reshape(FF, 3)),
        "convb": np.ascontiguousarray(inputs["L0_ffn_conv_b"].reshape(FF, 1)),
        "w2": np.ascontiguousarray(inputs["L0_ffn_w2"]),
    }
    base.update(consts)
    nc = build_program()
    in_maps = []
    for c in range(8):
        m = dict(base)
        m["x_in"] = np.ascontiguousarray(x[2 * c:2 * c + 2].transpose(0, 2, 1))
        in_maps.append(m)
    res = run_bass_kernel_spmd(nc, in_maps, core_ids=list(range(8)))
    out = np.concatenate([np.asarray(r["x_out"]).transpose(0, 2, 1) for r in res.results], axis=0)
    return np.ascontiguousarray(out)
